# version 01 (final)
# baseline (speedup 1.0000x reference)
import numpy as np
import ml_dtypes
from contextlib import ExitStack

import concourse.bass as bass
import concourse.mybir as mybir
from concourse.bass_utils import run_bass_kernel_spmd

F32 = mybir.dt.float32
BF16 = mybir.dt.bfloat16
I32 = mybir.dt.int32
AF = mybir.ActivationFunctionType
ALU = mybir.AluOpType
AX = mybir.AxisListType.X
BF = ml_dtypes.bfloat16

NCORES = 8
T = 8192
R = 1024
D = 4096
EPS = 1e-5
SCALE = 1.0 / np.sqrt(128.0)
NEG = -1.0e30


class Ser:

    def __init__(self):
        self.ops = []
        self.total = 0

    def _add(self, eng, fn, inc):
        self.ops.append((eng, fn, self.total, inc))
        self.total += inc

    def dma(self, dst, src, **kw):
        self._add('sync', lambda e, d=dst, s=src, kw=kw: e.dma_start(out=d, in_=s, **kw), 16)

    def t(self, fn):
        self._add('tensor', fn, 1)

    def v(self, fn):
        self._add('vector', fn, 1)

    def s(self, fn):
        self._add('scalar', fn, 1)


def finish(nc, ser, st):
    sem = st.enter_context(nc.semaphore("ser"))
    block = st.enter_context(nc.Block())
    total = ser.total
    for eng in ('sync', 'tensor', 'vector', 'scalar'):
        ops = [(f, w, i) for (e, f, w, i) in ser.ops if e == eng]

        def mk(ops=ops):
            def body(engobj):
                for (f, w, i) in ops:
                    if w:
                        engobj.wait_ge(sem, w)
                    inst = f(engobj)
                    inst.then_inc(sem, i)
                engobj.wait_ge(sem, total)
            return body
        getattr(block, eng)(mk())


def launch(build_fn, in_maps):
    nc = bass.Bass("TRN2", target_bir_lowering=False)
    ser = Ser()
    with ExitStack() as st:
        build_fn(nc, ser, st)
        finish(nc, ser, st)
    res = run_bass_kernel_spmd(nc, in_maps, core_ids=list(range(NCORES)))
    return res.results


def sb(nc, st, name, shape, dt):
    return st.enter_context(nc.sbuf_tensor(name, shape, dt))


def ps(nc, st, name, shape, dt=F32):
    return st.enter_context(nc.psum_tensor(name, shape, dt))


def emit_ln(nc, ser, in_a, scr, stat, out_ap):
    ser.v(lambda e, a=in_a, s=stat: e.reduce_sum(s[:, 0:1], a, AX))
    ser.s(lambda e, s=stat: e.mul(s[:, 1:2], s[:, 0:1], 1.0 / D))
    ser.v(lambda e, a=in_a, sc=scr, s=stat: e.tensor_scalar(sc, a, s[:, 1:2], None, ALU.subtract))
    ser.s(lambda e, a=in_a, sc=scr: e.activation(a, sc, AF.Square))
    ser.v(lambda e, a=in_a, s=stat: e.reduce_sum(s[:, 2:3], a, AX))
    ser.s(lambda e, s=stat: e.activation(s[:, 3:4], s[:, 2:3], AF.Sqrt, bias=s[:, 5:6], scale=1.0 / D))
    ser.v(lambda e, s=stat: e.reciprocal(s[:, 4:5], s[:, 3:4]))
    ser.v(lambda e, sc=scr, s=stat, o=out_ap: e.tensor_scalar(o, sc, s[:, 4:5], None, ALU.mult))



def build_prep(nc, ser, st):
    gqw = nc.dram_tensor("gqw", [4096, 172], I32, kind="ExternalInput").ap()
    gsc = nc.dram_tensor("gsc", [32, 1376], F32, kind="ExternalInput").ap()
    gqz = nc.dram_tensor("gqz", [32, 172], I32, kind="ExternalInput").ap()
    uqw = nc.dram_tensor("uqw", [4096, 172], I32, kind="ExternalInput").ap()
    usc = nc.dram_tensor("usc", [32, 1376], F32, kind="ExternalInput").ap()
    uqz = nc.dram_tensor("uqz", [32, 172], I32, kind="ExternalInput").ap()
    dqw = nc.dram_tensor("dqw", [11008, 64], I32, kind="ExternalInput").ap()
    dsc = nc.dram_tensor("dsc", [86, 512], F32, kind="ExternalInput").ap()
    dqz = nc.dram_tensor("dqz", [86, 64], I32, kind="ExternalInput").ap()
    wg_o = nc.dram_tensor("wg_o", [4096, 1376], BF16, kind="ExternalOutput").ap()
    wu_o = nc.dram_tensor("wu_o", [4096, 1376], BF16, kind="ExternalOutput").ap()
    wd_o = nc.dram_tensor("wd_o", [11008, 512], BF16, kind="ExternalOutput").ap()
    casts = [("c0", 4096, 768), ("c1", 4096, 512), ("c2", 4096, 1536),
             ("c3", 4096, 512), ("c4", 512, 1024), ("c5", 2048, 256)]
    cin = {n: nc.dram_tensor(n, [r, c], F32, kind="ExternalInput").ap() for (n, r, c) in casts}
    cout = {n: nc.dram_tensor(n + "o", [r, c], BF16, kind="ExternalOutput").ap() for (n, r, c) in casts}
    perm_d = nc.dram_tensor("perm", [1, 4096], I32, kind="ExternalInput").ap()
    iota_c = nc.dram_tensor("iota_c", [4, 128, 1], F32, kind="ExternalInput").ap()
    permrows = nc.dram_tensor("permrows", [4, 128, 1], I32, kind="ExternalInput").ap()
    iota_row = nc.dram_tensor("iota_row", [1, 4096], F32, kind="ExternalInput").ap()
    p_o = nc.dram_tensor("p_o", [512, 4096], BF16, kind="ExternalOutput").ap()
    pt_o = nc.dram_tensor("pt_o", [512, 4096], BF16, kind="ExternalOutput").ap()

    qw_sb = sb(nc, st, "qw_sb", [128, 172], I32)
    qn = sb(nc, st, "qn", [128, 1376], I32)
    qf = sb(nc, st, "qf", [128, 1376], F32)
    zrow = sb(nc, st, "zrow", [1, 172], I32)
    zn = sb(nc, st, "zn", [1, 1376], I32)
    zf = sb(nc, st, "zf", [1, 1376], F32)
    srow = sb(nc, st, "srow", [1, 1376], F32)
    zs = sb(nc, st, "zs", [1, 1376], F32)
    wbf = sb(nc, st, "wbf", [128, 1376], BF16)
    tmpc = sb(nc, st, "tmpc", [128, 512], F32)
    ones = sb(nc, st, "ones", [1, 128], F32)
    castf = sb(nc, st, "castf", [128, 2048], F32)
    castb = sb(nc, st, "castb", [128, 2048], BF16)
    permi = sb(nc, st, "permi", [1, 4096], I32)
    pf = sb(nc, st, "pf", [1, 4096], F32)
    ior = sb(nc, st, "ior", [1, 4096], F32)
    iotab = sb(nc, st, "iotab", [128, 4096], F32)
    iocol = sb(nc, st, "iocol", [128, 1], F32)
    prow_i = sb(nc, st, "prow_i", [128, 1], I32)
    prow_f = sb(nc, st, "prow_f", [128, 1], F32)
    pbf = sb(nc, st, "pbf", [128, 4096], BF16)
    pb1 = ps(nc, st, "pb1", [128, 512])
    pb2 = ps(nc, st, "pb2", [128, 512])

    ser.v(lambda e: e.memset(ones[:], 1.0))

    def dequant(qw_d, sc_d, qz_d, out_d, K, NC):
        ncp = NC // 8
        for kt in range(K // 128):
            ser.dma(qw_sb[:, :ncp], qw_d[kt * 128:(kt + 1) * 128, :])
            for j in range(8):
                ser.v(lambda e, j=j: e.tensor_scalar(
                    qn[:, j:NC:8], qw_sb[:, :ncp], 4 * j, 15,
                    ALU.logical_shift_right, ALU.bitwise_and))
            ser.v(lambda e: e.tensor_copy(qf[:, :NC], qn[:, :NC]))
            ser.dma(zrow[:, :ncp], qz_d[kt:kt + 1, :])
            for j in range(8):
                ser.v(lambda e, j=j: e.tensor_scalar(
                    zn[:, j:NC:8], zrow[:, :ncp], 4 * j, 15,
                    ALU.logical_shift_right, ALU.bitwise_and))
            ser.v(lambda e: e.tensor_copy(zf[:, :NC], zn[:, :NC]))
            ser.dma(srow[:, :NC], sc_d[kt:kt + 1, :])
            ser.v(lambda e: e.tensor_tensor(zs[:, :NC], zf[:, :NC], srow[:, :NC], ALU.mult))
            for ch in range(0, NC, 512):
                w = min(512, NC - ch)
                ser.t(lambda e, ch=ch, w=w: e.matmul(pb1[:, :w], ones[:], srow[:, ch:ch + w], start=True, stop=True))
                ser.t(lambda e, ch=ch, w=w: e.matmul(pb2[:, :w], ones[:], zs[:, ch:ch + w], start=True, stop=True))
                ser.v(lambda e, ch=ch, w=w: e.tensor_tensor(tmpc[:, :w], qf[:, ch:ch + w], pb1[:, :w], ALU.mult))
                ser.v(lambda e, ch=ch, w=w: e.tensor_tensor(wbf[:, ch:ch + w], tmpc[:, :w], pb2[:, :w], ALU.subtract))
            ser.dma(out_d[kt * 128:(kt + 1) * 128, :], wbf[:, :NC])

    dequant(gqw, gsc, gqz, wg_o, 4096, 1376)
    dequant(uqw, usc, uqz, wu_o, 4096, 1376)
    dequant(dqw, dsc, dqz, wd_o, 11008, 512)

    for (n, rows, cols) in casts:
        for rt in range(rows // 128):
            ser.dma(castf[:, :cols], cin[n][rt * 128:(rt + 1) * 128, :])
            ser.s(lambda e, cols=cols: e.copy(castb[:, :cols], castf[:, :cols]))
            ser.dma(cout[n][rt * 128:(rt + 1) * 128, :], castb[:, :cols])

    ser.dma(permi[:], perm_d[:, :])
    ser.v(lambda e: e.tensor_copy(pf[:], permi[:]))
    for t in range(4):
        ser.dma(iocol[:], iota_c[t])
        for ch in range(8):
            ser.t(lambda e, ch=ch: e.matmul(pb1[:], ones[:], pf[:, ch * 512:(ch + 1) * 512], start=True, stop=True))
            ser.v(lambda e, ch=ch: e.tensor_scalar(
                pbf[:, ch * 512:(ch + 1) * 512], pb1[:], iocol[:], None, ALU.is_equal))
        ser.dma(p_o[t * 128:(t + 1) * 128, :], pbf[:])
    ser.dma(ior[:], iota_row[:, :])
    for ch in range(8):
        ser.t(lambda e, ch=ch: e.matmul(pb1[:], ones[:], ior[:, ch * 512:(ch + 1) * 512], start=True, stop=True))
        ser.s(lambda e, ch=ch: e.copy(iotab[:, ch * 512:(ch + 1) * 512], pb1[:]))
    for t in range(4):
        ser.dma(prow_i[:], permrows[t])
        ser.v(lambda e: e.tensor_copy(prow_f[:], prow_i[:]))
        for ch in range(8):
            ser.v(lambda e, ch=ch: e.tensor_scalar(
                pbf[:, ch * 512:(ch + 1) * 512], iotab[:, ch * 512:(ch + 1) * 512],
                prow_f[:], None, ALU.is_equal))
        ser.dma(pt_o[t * 128:(t + 1) * 128, :], pbf[:])



def build_ln0(nc, ser, st):
    x_d = nc.dram_tensor("x", [R, D], F32, kind="ExternalInput").ap()
    ln_d = nc.dram_tensor("ln", [R, D], BF16, kind="ExternalOutput").ap()
    a = sb(nc, st, "a", [128, D], F32)
    b = sb(nc, st, "b", [128, D], F32)
    stat = sb(nc, st, "stat", [128, 8], F32)
    lnbf = sb(nc, st, "lnbf", [128, D], BF16)
    ser.v(lambda e: e.memset(stat[:, 5:6], EPS))
    for r in range(R // 128):
        ser.dma(a[:], x_d[r * 128:(r + 1) * 128, :])
        emit_ln(nc, ser, a[:], b[:], stat, lnbf[:])
        ser.dma(ln_d[r * 128:(r + 1) * 128, :], lnbf[:])



def build_mm(nc, ser, st, cfg):
    K, N = cfg['K'], cfg['N']
    blocks = cfg.get('blocks')
    WN = cfg.get('WN', N)
    if blocks is None:
        blocks = [(0, K, 0, 0, N)]
    resident = cfg.get('resident', True)
    slab_nk = cfg.get('slab_nk', 0)
    lt_d = nc.dram_tensor("lt", [K, R], BF16, kind="ExternalInput").ap()
    w_d = nc.dram_tensor("w", [K, WN], BF16, kind="ExternalInput").ap()
    bias_d = None
    if cfg.get('bias'):
        bias_d = nc.dram_tensor("bias", [1, N], F32, kind="ExternalInput").ap()

    mode = cfg['mode']
    if mode == 'lnres':
        x_d = nc.dram_tensor("xin", [R, D], F32, kind="ExternalInput").ap()
        xo_d = nc.dram_tensor("xout", [R, D], F32, kind="ExternalOutput").ap()
        ln_d = None
        if cfg.get('ln_out', True):
            ln_d = nc.dram_tensor("lnout", [R, D], BF16, kind="ExternalOutput").ap()
        dacc = sb(nc, st, "dacc", [128, 8, D], BF16)
        a = sb(nc, st, "a", [128, D], F32)
        b = sb(nc, st, "b", [128, D], F32)
        c = sb(nc, st, "c", [128, D], F32)
        stat = sb(nc, st, "stat", [128, 8], F32)
        lnbf = sb(nc, st, "lnbf", [128, D], BF16)
        ser.v(lambda e: e.memset(stat[:, 5:6], EPS))
    else:
        out_dt = cfg.get('out_dt', BF16)
        o_d = nc.dram_tensor("o", [R, N], out_dt, kind="ExternalOutput").ap()
        osb = sb(nc, st, "osb", [128, 512], out_dt)

    if resident:
        lt_sb = sb(nc, st, "lt_sb", [128, K // 128, 1024], BF16)
        for k in range(K // 128):
            ser.dma(lt_sb[:, k, :], lt_d[k * 128:(k + 1) * 128, :])

        def lt_tile_ap(kidx, r):
            return lt_sb[:, kidx, r * 128:(r + 1) * 128]
    elif slab_nk:
        lt_slab = sb(nc, st, "lt_slab", [128, slab_nk, 1024], BF16)

        def lt_tile_ap(kidx, r):
            return lt_slab[:, kidx, r * 128:(r + 1) * 128]
    else:
        lt_tile = sb(nc, st, "lt_tile", [128, 1024], BF16)

        def lt_tile_ap(kidx, r):
            return lt_tile[:, r * 128:(r + 1) * 128]

    w_sb = sb(nc, st, "w_sb", [128, 512], BF16)
    psl = [ps(nc, st, f"ps{r}", [128, 512]) for r in range(8)]

    if bias_d is not None:
        bias_sb = sb(nc, st, "bias_sb", [128, N], F32)
        brow = sb(nc, st, "brow", [1, 512], F32)
        ones = sb(nc, st, "ones", [1, 128], F32)
        ser.v(lambda e: e.memset(ones[:], 1.0))
        for ch in range(0, N, 512):
            w = min(512, N - ch)
            ser.dma(brow[:, :w], bias_d[:, ch:ch + w])
            ser.t(lambda e, w=w: e.matmul(psl[0][:, :w], ones[:], brow[:, :w], start=True, stop=True))
            ser.s(lambda e, ch=ch, w=w: e.copy(bias_sb[:, ch:ch + w], psl[0][:, :w]))

    for (k0, k1, out_n0, w_col0, width) in blocks:
        nk = (k1 - k0) // 128
        if slab_nk and not resident:
            for ki in range(nk):
                ser.dma(lt_slab[:, ki, :], lt_d[k0 + ki * 128:k0 + (ki + 1) * 128, :])
        for ch in range(0, width, 512):
            w = min(512, width - ch)
            for ki in range(nk):
                k = k0 + ki * 128
                ser.dma(w_sb[:, :w], w_d[k:k + 128, w_col0 + ch:w_col0 + ch + w])
                if not resident and not slab_nk:
                    ser.dma(lt_tile[:], lt_d[k:k + 128, :])
                for r in range(8):
                    ser.t(lambda e, r=r, ki=ki, w=w, nk=nk: e.matmul(
                        psl[r][:, :w], lt_tile_ap(ki, r), w_sb[:, :w],
                        start=(ki == 0), stop=(ki == nk - 1)))
            gn = out_n0 + ch
            for r in range(8):
                if mode == 'lnres':
                    if bias_d is not None:
                        ser.v(lambda e, r=r, gn=gn, w=w: e.tensor_tensor(
                            dacc[:, r, gn:gn + w], psl[r][:, :w], bias_sb[:, gn:gn + w], ALU.add))
                    else:
                        ser.s(lambda e, r=r, gn=gn, w=w: e.copy(dacc[:, r, gn:gn + w], psl[r][:, :w]))
                else:
                    if bias_d is not None:
                        ser.v(lambda e, r=r, gn=gn, w=w: e.tensor_tensor(
                            osb[:, :w], psl[r][:, :w], bias_sb[:, gn:gn + w], ALU.add))
                    elif cfg.get('act') == 'gelu':
                        ser.s(lambda e, r=r, w=w: e.activation(osb[:, :w], psl[r][:, :w], AF.Gelu))
                    else:
                        ser.s(lambda e, r=r, w=w: e.copy(osb[:, :w], psl[r][:, :w]))
                    ser.dma(o_d[r * 128:(r + 1) * 128, gn:gn + w], osb[:, :w])

    if mode == 'lnres':
        for r in range(8):
            ser.s(lambda e, r=r: e.copy(a[:], dacc[:, r, :]))
            emit_ln(nc, ser, a[:], b[:], stat, c[:])
            ser.dma(a[:], x_d[r * 128:(r + 1) * 128, :])
            ser.v(lambda e: e.tensor_tensor(b[:], c[:], a[:], ALU.add))
            ser.dma(xo_d[r * 128:(r + 1) * 128, :], b[:])
            if ln_d is not None:
                emit_ln(nc, ser, b[:], a[:], stat, lnbf[:])
                ser.dma(ln_d[r * 128:(r + 1) * 128, :], lnbf[:])



def build_attn(nc, ser, st, rope_mode):
    q_d = nc.dram_tensor("q", [16, 2048, 128], F32, kind="ExternalInput").ap()
    k_d = nc.dram_tensor("k", [16, 2048, 128], F32, kind="ExternalInput").ap()
    v_d = nc.dram_tensor("v", [16, 2048, 128], F32, kind="ExternalInput").ap()
    cos_d = nc.dram_tensor("cos", [2048, 64], F32, kind="ExternalInput").ap()
    sin_d = nc.dram_tensor("sin", [2048, 64], F32, kind="ExternalInput").ap()
    tri_d = nc.dram_tensor("tri", [128, 128], F32, kind="ExternalInput").ap()
    idf_d = nc.dram_tensor("idf", [128, 128], F32, kind="ExternalInput").ap()
    idb_d = nc.dram_tensor("idb", [128, 128], BF16, kind="ExternalInput").ap()
    o_d = nc.dram_tensor("o", [16, 2048, 128], BF16, kind="ExternalOutput").ap()

    q3 = sb(nc, st, "q3", [128, 16, 128], F32)
    k3 = sb(nc, st, "k3", [128, 16, 128], F32)
    v3 = sb(nc, st, "v3", [128, 16, 128], F32)
    vb = sb(nc, st, "vb", [128, 16, 128], BF16)
    qT = sb(nc, st, "qT", [128, 2048], BF16)
    kT = sb(nc, st, "kT", [128, 2048], BF16)
    cos3 = sb(nc, st, "cos3", [128, 16, 64], F32)
    sin3 = sb(nc, st, "sin3", [128, 16, 64], F32)
    r1 = sb(nc, st, "r1", [128, 16, 64], F32)
    r2 = sb(nc, st, "r2", [128, 16, 64], F32)
    tA = sb(nc, st, "tA", [128, 16, 64], F32)
    tB = sb(nc, st, "tB", [128, 16, 64], F32)
    tri_sb = sb(nc, st, "tri_sb", [128, 128], F32)
    idf = sb(nc, st, "idf_sb", [128, 128], F32)
    idb = sb(nc, st, "idb_sb", [128, 128], BF16)
    ex = sb(nc, st, "ex", [128, 2048], BF16)
    exT = sb(nc, st, "exT", [128, 2048], BF16)
    nm0 = sb(nc, st, "nm0", [128, 1], F32)
    nm = sb(nc, st, "nm", [128, 1], F32)
    den = sb(nc, st, "den", [128, 1], F32)
    dinv = sb(nc, st, "dinv", [128, 1], F32)
    osb = sb(nc, st, "osb", [128, 128], BF16)

    ps_sc = ps(nc, st, "ps_sc", [128, 2048])
    ps_tr = ps(nc, st, "ps_tr", [128, 128])
    ps_trb = ps(nc, st, "ps_trb", [128, 128], BF16)
    ps_av = ps(nc, st, "ps_av", [128, 128])

    cr = cos_d.rearrange("(t p) f -> p t f", p=128)
    sr = sin_d.rearrange("(t p) f -> p t f", p=128)
    for j in range(4):
        ser.dma(cos3[:, j * 4:(j + 1) * 4, :], cr[:, j * 4:(j + 1) * 4, :])
        ser.dma(sin3[:, j * 4:(j + 1) * 4, :], sr[:, j * 4:(j + 1) * 4, :])
    ser.dma(tri_sb[:], tri_d[:, :])
    ser.dma(idf[:], idf_d[:, :])
    ser.dma(idb[:], idb_d[:, :])

    def rope(x3):
        if rope_mode == 'il':
            x1 = x3[:, :, 0:128:2]
            x2 = x3[:, :, 1:128:2]
        else:
            x1 = x3[:, :, 0:64]
            x2 = x3[:, :, 64:128]
        ser.v(lambda e: e.tensor_tensor(r1[:], x1, cos3[:], ALU.mult))
        ser.v(lambda e: e.tensor_tensor(tA[:], x2, sin3[:], ALU.mult))
        ser.v(lambda e: e.tensor_tensor(tB[:], r1[:], tA[:], ALU.subtract))
        ser.v(lambda e: e.tensor_tensor(r1[:], x1, sin3[:], ALU.mult))
        ser.v(lambda e: e.tensor_tensor(tA[:], x2, cos3[:], ALU.mult))
        ser.v(lambda e: e.tensor_tensor(r2[:], r1[:], tA[:], ALU.add))
        ser.s(lambda e: e.copy(x1, tB[:]))
        ser.s(lambda e: e.copy(x2, r2[:]))

    for i in range(16):
        for (dst, src) in ((q3, q_d), (k3, k_d), (v3, v_d)):
            rsrc = src[i].rearrange("(t p) d -> p t d", p=128)
            for j in range(4):
                ser.dma(dst[:, j * 4:(j + 1) * 4, :], rsrc[:, j * 4:(j + 1) * 4, :])
        rope(q3)
        rope(k3)
        ser.s(lambda e: e.copy(vb[:], v3[:]))
        for t in range(16):
            ser.t(lambda e, t=t: e.transpose(ps_tr[:], q3[:, t, :], idf[:]))
            ser.s(lambda e, t=t: e.copy(qT[:, t * 128:(t + 1) * 128], ps_tr[:]))
        for t in range(16):
            ser.t(lambda e, t=t: e.transpose(ps_tr[:], k3[:, t, :], idf[:]))
            ser.s(lambda e, t=t: e.copy(kT[:, t * 128:(t + 1) * 128], ps_tr[:]))
        for j in range(16):
            Lj = (j + 1) * 128
            for tc in range(0, Lj, 512):
                w = min(512, Lj - tc)
                ser.t(lambda e, j=j, tc=tc, w=w: e.matmul(
                    ps_sc[:, tc:tc + w], qT[:, j * 128:(j + 1) * 128],
                    kT[:, tc:tc + w], start=True, stop=True))
            ser.v(lambda e, j=j: e.tensor_tensor(
                ps_sc[:, j * 128:(j + 1) * 128], ps_sc[:, j * 128:(j + 1) * 128],
                tri_sb[:], ALU.add))
            ser.v(lambda e, Lj=Lj: e.reduce_max(nm0[:], ps_sc[:, :Lj], AX))
            ser.s(lambda e: e.mul(nm[:], nm0[:], -SCALE))
            ser.s(lambda e, Lj=Lj: e.activation(ex[:, :Lj], ps_sc[:, :Lj], AF.Exp,
                                                bias=nm[:], scale=SCALE))
            ser.v(lambda e, Lj=Lj: e.reduce_sum(den[:], ex[:, :Lj], AX))
            ser.v(lambda e: e.reciprocal(dinv[:], den[:]))
            for t in range(j + 1):
                ser.t(lambda e, t=t: e.transpose(ps_trb[:], ex[:, t * 128:(t + 1) * 128], idb[:]))
                ser.v(lambda e, t=t: e.tensor_copy(exT[:, t * 128:(t + 1) * 128], ps_trb[:]))
            for t in range(j + 1):
                ser.t(lambda e, t=t, j=j: e.matmul(
                    ps_av[:], exT[:, t * 128:(t + 1) * 128], vb[:, t, :],
                    start=(t == 0), stop=(t == j)))
            ser.v(lambda e: e.tensor_scalar(osb[:], ps_av[:], dinv[:], None, ALU.mult))
            ser.dma(o_d[i, j * 128:(j + 1) * 128, :], osb[:])



def build_silu_mul(nc, ser, st):
    N = 11008
    g_d = nc.dram_tensor("g", [R, N], BF16, kind="ExternalInput").ap()
    u_d = nc.dram_tensor("u", [R, N], BF16, kind="ExternalInput").ap()
    h_d = nc.dram_tensor("h", [R, N], BF16, kind="ExternalOutput").ap()
    gb = sb(nc, st, "gb", [128, N], BF16)
    ub = sb(nc, st, "ub", [128, N], BF16)
    A = sb(nc, st, "A", [128, N], F32)
    Bb = sb(nc, st, "Bb", [128, N], F32)
    hb = sb(nc, st, "hb", [128, N], BF16)
    for r in range(8):
        ser.dma(gb[:], g_d[r * 128:(r + 1) * 128, :])
        ser.dma(ub[:], u_d[r * 128:(r + 1) * 128, :])
        ser.s(lambda e: e.activation(A[:], gb[:], AF.Silu))
        ser.s(lambda e: e.copy(Bb[:], ub[:]))
        ser.v(lambda e: e.tensor_tensor(hb[:], A[:], Bb[:], ALU.mult))
        ser.dma(h_d[r * 128:(r + 1) * 128, :], hb[:])



def _shard_rows(arr):
    return [np.ascontiguousarray(arr[c * R:(c + 1) * R]) for c in range(NCORES)]


def _t(arr):
    return np.ascontiguousarray(arr.T)


def _gather_rows(res, key):
    return np.concatenate([r[key] for r in res], axis=0)


def run_mm(cfg, lhsT, w, bias=None, x_in=None):
    maps = []
    for c in range(NCORES):
        m = {"lt": np.ascontiguousarray(lhsT[:, c * R:(c + 1) * R]), "w": w}
        if bias is not None:
            m["bias"] = bias.reshape(1, -1).astype(np.float32)
        if x_in is not None:
            m["xin"] = np.ascontiguousarray(x_in[c * R:(c + 1) * R])
        maps.append(m)
    return launch(lambda nc, ser, st: build_mm(nc, ser, st, cfg), maps)


def run_attn(qkv, qcol, kcol, vcol, kv_rep, rope_mode, consts):
    maps = []
    for c in range(NCORES):
        qs, ks, vs = [], [], []
        for ii in range(16):
            idx = c * 16 + ii
            b, h = idx // 32, idx % 32
            g = h // kv_rep
            qs.append(qkv[b, :, qcol + h * 128:qcol + (h + 1) * 128])
            ks.append(qkv[b, :, kcol + g * 128:kcol + (g + 1) * 128])
            vs.append(qkv[b, :, vcol + g * 128:vcol + (g + 1) * 128])
        m = {"q": np.ascontiguousarray(np.stack(qs)),
             "k": np.ascontiguousarray(np.stack(ks)),
             "v": np.ascontiguousarray(np.stack(vs))}
        m.update(consts)
        maps.append(m)
    res = launch(lambda nc, ser, st: build_attn(nc, ser, st, rope_mode), maps)
    ctx = np.empty((T, D), dtype=BF)
    for c in range(NCORES):
        o = res[c]["o"]
        for ii in range(16):
            idx = c * 16 + ii
            b, h = idx // 32, idx % 32
            ctx[b * 2048:(b + 1) * 2048, h * 128:(h + 1) * 128] = o[ii]
    return ctx


def kernel(**inp):
    x = np.asarray(inp["L0_attn_x"], np.float32).reshape(T, D)
    perm = np.asarray(inp["L1_ffn_perm"], np.int32)

    gqw = np.asarray(inp["L0_ffn_gate_qweight"], np.int32)
    gsc = np.asarray(inp["L0_ffn_gate_scales"], np.float32)
    gqz = np.asarray(inp["L0_ffn_gate_qzeros"], np.int32)
    uqw = np.asarray(inp["L0_ffn_up_qweight"], np.int32)
    usc = np.asarray(inp["L0_ffn_up_scales"], np.float32)
    uqz = np.asarray(inp["L0_ffn_up_qzeros"], np.int32)
    dqw = np.asarray(inp["L0_ffn_down_qweight"], np.int32)
    dsc = np.asarray(inp["L0_ffn_down_scales"], np.float32)
    dqz = np.asarray(inp["L0_ffn_down_qzeros"], np.int32)
    wqkv0 = np.asarray(inp["L0_attn_w_qkv"], np.float32)
    wo0 = np.asarray(inp["L0_attn_w_o"], np.float32)
    wqkv1 = np.asarray(inp["L1_attn_qkv_w"], np.float32)
    wo1 = np.asarray(inp["L1_attn_out_w"], np.float32)
    up1 = np.asarray(inp["L1_ffn_up"], np.float32).reshape(4096, 1024)
    dn1 = np.asarray(inp["L1_ffn_down"], np.float32).reshape(16384, 256)
    iota_row = np.arange(4096, dtype=np.float32).reshape(1, 4096)

    maps = []
    for c in range(NCORES):
        maps.append({
            "gqw": np.ascontiguousarray(gqw[:, c * 172:(c + 1) * 172]),
            "gsc": np.ascontiguousarray(gsc[:, c * 1376:(c + 1) * 1376]),
            "gqz": np.ascontiguousarray(gqz[:, c * 172:(c + 1) * 172]),
            "uqw": np.ascontiguousarray(uqw[:, c * 172:(c + 1) * 172]),
            "usc": np.ascontiguousarray(usc[:, c * 1376:(c + 1) * 1376]),
            "uqz": np.ascontiguousarray(uqz[:, c * 172:(c + 1) * 172]),
            "dqw": np.ascontiguousarray(dqw[:, c * 64:(c + 1) * 64]),
            "dsc": np.ascontiguousarray(dsc[:, c * 512:(c + 1) * 512]),
            "dqz": np.ascontiguousarray(dqz[:, c * 64:(c + 1) * 64]),
            "c0": np.ascontiguousarray(wqkv0[:, c * 768:(c + 1) * 768]),
            "c1": np.ascontiguousarray(wo0[:, c * 512:(c + 1) * 512]),
            "c2": np.ascontiguousarray(wqkv1[:, c * 1536:(c + 1) * 1536]),
            "c3": np.ascontiguousarray(wo1[:, c * 512:(c + 1) * 512]),
            "c4": np.ascontiguousarray(up1[c * 512:(c + 1) * 512]),
            "c5": np.ascontiguousarray(dn1[c * 2048:(c + 1) * 2048]),
            "perm": perm.reshape(1, 4096),
            "iota_c": np.arange(c * 512, (c + 1) * 512, dtype=np.float32).reshape(4, 128, 1),
            "permrows": np.ascontiguousarray(perm[c * 512:(c + 1) * 512]).reshape(4, 128, 1),
            "iota_row": iota_row,
        })
    res = launch(build_prep, maps)
    Wg = np.concatenate([r["wg_o"] for r in res], axis=1)
    Wu = np.concatenate([r["wu_o"] for r in res], axis=1)
    Wd = np.concatenate([r["wd_o"] for r in res], axis=1)
    Wqkv0 = np.concatenate([r["c0o"] for r in res], axis=1)
    Wo0 = np.concatenate([r["c1o"] for r in res], axis=1)
    Wqkv1 = np.concatenate([r["c2o"] for r in res], axis=1)
    Wo1 = np.concatenate([r["c3o"] for r in res], axis=1)
    Up1 = np.concatenate([r["c4o"] for r in res], axis=0)
    Dn1 = np.concatenate([r["c5o"] for r in res], axis=0)
    P = np.concatenate([r["p_o"] for r in res], axis=0)
    PT = np.concatenate([r["pt_o"] for r in res], axis=0)

    inv_freq = (1.0 / (10000.0 ** (np.arange(0, 64, dtype=np.float64) / 64.0)))
    ang = np.arange(2048, dtype=np.float64)[:, None] * inv_freq[None, :]
    cos_t = np.cos(ang).astype(np.float32)
    sin_t = np.sin(ang).astype(np.float32)
    tri = (np.triu(np.ones((128, 128), np.float32), k=1) * NEG).astype(np.float32)
    idf = np.eye(128, dtype=np.float32)
    idb = np.eye(128).astype(BF)
    consts = {"cos": cos_t, "sin": sin_t, "tri": tri, "idf": idf, "idb": idb}

    res = launch(build_ln0, [{"x": s} for s in _shard_rows(x)])
    ln0 = _gather_rows(res, "ln")
    res = run_mm(dict(K=4096, N=6144, mode='plain', out_dt=F32, resident=True),
                 _t(ln0), Wqkv0)
    qkv0 = _gather_rows(res, "o")
    ctx0 = run_attn(qkv0.reshape(4, 2048, 6144), 0, 4096, 5120, 4, 'il', consts)
    res = run_mm(dict(K=4096, N=4096, mode='lnres', resident=False),
                 _t(ctx0), Wo0, x_in=x)
    x1 = _gather_rows(res, "xout")
    ln1 = _gather_rows(res, "lnout")

    ln1T = _t(ln1)
    res = run_mm(dict(K=4096, N=11008, mode='plain', resident=True), ln1T, Wg)
    g = _gather_rows(res, "o")
    res = run_mm(dict(K=4096, N=11008, mode='plain', resident=True), ln1T, Wu)
    u = _gather_rows(res, "o")
    maps = [{"g": gs, "u": us} for gs, us in zip(_shard_rows(g), _shard_rows(u))]
    res = launch(build_silu_mul, maps)
    h = _gather_rows(res, "h")
    res = run_mm(dict(K=11008, N=4096, mode='lnres', resident=False),
                 _t(h), Wd, x_in=x1)
    x2 = _gather_rows(res, "xout")
    ln2 = _gather_rows(res, "lnout")

    res = run_mm(dict(K=4096, N=12288, mode='plain', out_dt=F32, resident=True, bias=True),
                 _t(ln2), Wqkv1, bias=np.asarray(inp["L1_attn_qkv_b"], np.float32))
    qkv1 = _gather_rows(res, "o")
    ctx1 = run_attn(qkv1.reshape(4, 2048, 12288), 0, 4096, 8192, 1, 'half', consts)
    res = run_mm(dict(K=4096, N=4096, mode='lnres', resident=False, bias=True),
                 _t(ctx1), Wo1, bias=np.asarray(inp["L1_attn_out_b"], np.float32), x_in=x2)
    x3 = _gather_rows(res, "xout")
    ln3 = _gather_rows(res, "lnout")

    res = run_mm(dict(K=4096, N=4096, mode='plain', resident=True), _t(ln3), P)
    tp = _gather_rows(res, "o")
    blocks_up = [(hh * 256, (hh + 1) * 256, hh * 1024, 0, 1024) for hh in range(16)]
    res = run_mm(dict(K=4096, N=16384, WN=1024, mode='plain', act='gelu',
                      resident=False, slab_nk=2, blocks=blocks_up), _t(tp), Up1)
    hg = _gather_rows(res, "o")
    blocks_dn = [(hh * 1024, (hh + 1) * 1024, hh * 256, 0, 256) for hh in range(16)]
    res = run_mm(dict(K=16384, N=4096, WN=256, mode='plain',
                      resident=False, slab_nk=8, blocks=blocks_dn), _t(hg), Dn1)
    z = _gather_rows(res, "o")
    res = run_mm(dict(K=4096, N=4096, mode='lnres', ln_out=False, resident=False),
                 _t(z), PT, x_in=x3)
    x4 = _gather_rows(res, "xout")

    return x4.reshape(4, 2048, 4096).astype(np.float32)
